# version 03
# baseline (speedup 1.0000x reference)
import numpy as np
from contextlib import ExitStack

import concourse.bass as bass
import concourse.mybir as mybir
from concourse.bass_utils import run_bass_kernel_spmd

F32 = mybir.dt.float32
BF16 = mybir.dt.bfloat16
U8 = mybir.dt.uint8
AF = mybir.ActivationFunctionType
ALU = mybir.AluOpType
AX = mybir.AxisListType

D_MODEL = 2048
L_SEQ = 4096
N_ST = 16
T_SC = 512
N_CHUNK = L_SEQ // T_SC

CONSTS = [0.0, 1.0, 0.5, -0.5, 2.0, -1.0, 1e-5, -20.0,
          0.08838834764831845, 1.0 / 2048.0, -128.0, -2.0]


class Sems:

    def __init__(self, es, nc, names):
        self.h = {}
        self.c = {}
        for n in names:
            self.h[n] = es.enter_context(nc.semaphore(n))
            self.c[n] = 0

    def inc(self, ins, name, v=1):
        ins.then_inc(self.h[name], v)
        self.c[name] += v
        return self.c[name]

    def val(self, name):
        return self.c[name]


def setup_consts(nc, es):
    ct = es.enter_context(nc.sbuf_tensor("consts", [128, len(CONSTS)], F32))
    for j, val in enumerate(CONSTS):
        nc.const_aps.register(F32, val, ct[:, j:j + 1])

    def emit(v):
        for j, val in enumerate(CONSTS):
            v.memset(ct[:, j:j + 1], val)
    return emit



def build_scan(G=8):
    nc = bass.Bass("TRN2", target_bir_lowering=False)
    DH = G * 128

    xT = nc.dram_tensor("xT", [D_MODEL, L_SEQ], F32, kind="ExternalInput").ap()
    xTh = nc.dram_tensor("xTh", [DH, L_SEQ], F32, kind="ExternalInput").ap()
    xprojT = nc.dram_tensor("xprojT", [D_MODEL, 160], F32, kind="ExternalInput").ap()
    dtwT = nc.dram_tensor("dtwT", [128, DH], F32, kind="ExternalInput").ap()
    a_log = nc.dram_tensor("a_log", [128, G * 16], F32, kind="ExternalInput").ap()
    d_prm = nc.dram_tensor("d_prm", [128, G], F32, kind="ExternalInput").ap()
    b_dt = nc.dram_tensor("b_dt", [128, G], F32, kind="ExternalInput").ap()

    y_out = nc.dram_tensor("y_out", [DH, L_SEQ], F32, kind="ExternalOutput").ap()

    bc_dram = nc.dram_tensor("bc_dram", [32, T_SC], F32, kind="Internal").ap()

    es = ExitStack()
    with es:
        sb = lambda n, sh, dt: es.enter_context(nc.sbuf_tensor(n, sh, dt))
        ps = lambda n, sh, dt: es.enter_context(nc.psum_tensor(n, sh, dt))

        xTb = sb("xTb", [128, 16 * T_SC], F32)
        xTh_s = sb("xTh_s", [128, G * T_SC], F32)
        xprojs = sb("xprojs", [128, 16 * 160], F32)
        dtws = sb("dtws", [128, DH], F32)
        a_neg = sb("a_neg", [128, G * 16], F32)
        d_s = sb("d_s", [128, G], F32)
        b_s = sb("b_s", [128, G], F32)
        dtT = sb("dtT", [128, T_SC], F32)
        bcc = sb("bcc", [32, T_SC], F32)
        brep = sb("brep", [128, 16 * T_SC], F32)
        crep = sb("crep", [128, 16 * T_SC], F32)
        delta = [sb(f"delta{i}", [128, T_SC], F32) for i in range(2)]
        etmp = [sb(f"etmp{i}", [128, T_SC], F32) for i in range(2)]
        u_g = [sb(f"u{i}", [128, T_SC], F32) for i in range(2)]
        dA = [sb(f"dA{i}", [128, T_SC], F32) for i in range(2)]
        dBx = [sb(f"dBx{i}", [128, T_SC], F32) for i in range(2)]
        h_b = [sb(f"h{i}", [128, T_SC], F32) for i in range(2)]
        hc = sb("hc", [128, T_SC], F32)
        y_g = [sb(f"y{i}", [128, T_SC], F32) for i in range(2)]
        carry = sb("carry", [128, G * 16], F32)

        p_dt = ps("p_dt", [128, T_SC], F32)
        p_bc = ps("p_bc", [32, T_SC], F32)
        p_z = [ps(f"p_z{i}", [128, T_SC], F32) for i in range(2)]

        sm = Sems(es, nc, ["ld", "mm", "sc", "ve", "gp", "st", "bc"])
        M = {}

        consts_emit = setup_consts(nc, es)

        with nc.Block() as blk:
            @blk.vector
            def _(v):
                consts_emit(v)
                v.memset(carry[:], 0.0)
                sm.inc(v.memset(hc[:], 0.0), "ve")

            prep_ve = sm.val("ve")

            @blk.sync
            def _(s):
                sm.inc(s.dma_start(
                    out=xprojs[:].rearrange("p (g e) -> p g e", e=160),
                    in_=xprojT.rearrange("(g p) e -> p g e", p=128)), "ld", 16)
                sm.inc(s.dma_start(out=dtws[:], in_=dtwT), "ld", 16)
                sm.inc(s.dma_start(out=a_neg[:], in_=a_log), "ld", 16)
                sm.inc(s.dma_start(out=d_s[:], in_=d_prm), "ld", 16)
                sm.inc(s.dma_start(out=b_s[:], in_=b_dt), "ld", 16)

            prep_ld = sm.val("ld")

            @blk.scalar
            def _(s):
                s.wait_ge(sm.h["ld"], prep_ld)
                s.wait_ge(sm.h["ve"], prep_ve)
                sm.inc(s.activation(out=a_neg[:], in_=a_neg[:], func=AF.Exp), "sc")

            prep_sc = sm.val("sc")

            @blk.vector
            def _(v):
                v.wait_ge(sm.h["sc"], prep_sc)
                sm.inc(v.tensor_scalar_mul(a_neg[:], a_neg[:], -1.0), "ve")

            prep_ve2 = sm.val("ve")

            for c in range(N_CHUNK):
                @blk.sync
                def _(s, c=c):
                    if c >= 1:
                        s.wait_ge(sm.h["mm"], M[("mmx", c - 1)])
                        s.wait_ge(sm.h["ve"], M[("vend", c - 1)])
                        s.wait_ge(sm.h["sc"], M[("scend", c - 1)])
                        s.wait_ge(sm.h["gp"], M[("gpend", c - 1)])
                    sm.inc(s.dma_start(
                        out=xTb[:].rearrange("p (g t) -> p g t", t=T_SC),
                        in_=xT.rearrange("(g p) t -> p g t", p=128)[:, :, c * T_SC:(c + 1) * T_SC]),
                        "ld", 16)
                    sm.inc(s.dma_start(
                        out=xTh_s[:].rearrange("p (g t) -> p g t", t=T_SC),
                        in_=xTh.rearrange("(g p) t -> p g t", p=128)[:, :, c * T_SC:(c + 1) * T_SC]),
                        "ld", 16)

                M[("ldx", c)] = sm.val("ld")

                @blk.tensor
                def _(t, c=c):
                    t.wait_ge(sm.h["ld"], M[("ldx", c)])
                    if c == 0:
                        t.wait_ge(sm.h["ve"], prep_ve2)
                    else:
                        t.wait_ge(sm.h["sc"], M[("scdt", c - 1)])
                    for k in range(16):
                        t.matmul(p_dt[:], xprojs[:, k * 160:k * 160 + 128],
                                 xTb[:, k * T_SC:(k + 1) * T_SC],
                                 start=(k == 0), stop=(k == 15))
                    for k in range(16):
                        ins = t.matmul(p_bc[:], xprojs[:, k * 160 + 128:(k + 1) * 160],
                                       xTb[:, k * T_SC:(k + 1) * T_SC],
                                       start=(k == 0), stop=(k == 15))
                    sm.inc(ins, "mm")

                M[("mmx", c)] = sm.val("mm")

                @blk.scalar
                def _(s, c=c):
                    s.wait_ge(sm.h["mm"], M[("mmx", c)])
                    if c >= 1:
                        s.wait_ge(sm.h["st"], M[("stbc", c - 1)])
                        s.wait_ge(sm.h["mm"], M[("mmz", c - 1)])
                    s.copy(dtT[:], p_dt[:])
                    sm.inc(s.copy(bcc[:], p_bc[:]), "sc")

                M[("scdt", c)] = sm.val("sc")

                @blk.sync
                def _(s, c=c):
                    s.wait_ge(sm.h["sc"], M[("scdt", c)])
                    if c >= 1:
                        s.wait_ge(sm.h["bc"], M[("bcr", c - 1)])
                    sm.inc(s.dma_start(out=bc_dram, in_=bcc[:]), "st", 16)
                    M[("stbc", c)] = sm.val("st")
                    s.wait_ge(sm.h["st"], M[("stbc", c)])
                    if c >= 1:
                        s.wait_ge(sm.h["gp"], M[("gpend", c - 1)])
                        s.wait_ge(sm.h["ve"], M[("vend", c - 1)])
                    sm.inc(s.dma_start(
                        out=brep[:].rearrange("p (n t) -> p n t", t=T_SC),
                        in_=bass.AP(bc_dram.tensor, 0, [[0, 128], [T_SC, 16], [1, T_SC]])),
                        "bc", 16)
                    sm.inc(s.dma_start(
                        out=crep[:].rearrange("p (n t) -> p n t", t=T_SC),
                        in_=bass.AP(bc_dram.tensor, 16 * T_SC, [[0, 128], [T_SC, 16], [1, T_SC]])),
                        "bc", 16)

                M[("bcr", c)] = sm.val("bc")

                for g in range(G):
                    gb = g % 2

                    @blk.tensor
                    def _(t, c=c, g=g, gb=gb):
                        if g < 2:
                            t.wait_ge(sm.h["sc"], M[("scdt", c)])
                        if (c, g) > (0, 1):
                            t.wait_ge(sm.h["sc"], M[("scz", c, g)])
                        ins = t.matmul(p_z[gb][:], dtws[:, g * 128:(g + 1) * 128],
                                       dtT[:], start=True, stop=True)
                        sm.inc(ins, "mm")

                    M[("mmz", c)] = sm.val("mm")
                    mm_z = sm.val("mm")

                    @blk.scalar
                    def _(s, c=c, g=g, gb=gb):
                        s.wait_ge(sm.h["mm"], mm_z)
                        if (c, g) > (0, 1):
                            s.wait_ge(sm.h["ve"], M[("veu", c, g)])
                        s.activation(out=etmp[gb][:], in_=p_z[gb][:], func=AF.Exp,
                                     bias=b_s[:, g:g + 1], scale=1.0)
                        sm.inc(s.activation(out=delta[gb][:], in_=etmp[gb][:],
                                            func=AF.Ln, bias=1.0, scale=1.0), "sc")
                        M[("scz_done", c, g)] = sm.val("sc")
                        s.wait_ge(sm.h["st"], M.get(("sty", c, g), 0))
                        sm.inc(s.activation(out=y_g[gb][:],
                                            in_=xTh_s[:, g * T_SC:(g + 1) * T_SC],
                                            func=AF.Copy, scale=d_s[:, g:g + 1]), "sc")

                    if g + 2 < G:
                        nk = (c, g + 2)
                    else:
                        nk = (c + 1, g + 2 - G)
                    M[("scz",) + nk] = sm.val("sc")
                    sc_delta = M[("scz_done", c, g)]
                    sc_yinit = sm.val("sc")

                    @blk.vector
                    def _(v, c=c, g=g, gb=gb):
                        v.wait_ge(sm.h["sc"], sc_delta)
                        sm.inc(v.tensor_mul(u_g[gb][:], delta[gb][:],
                                            xTh_s[:, g * T_SC:(g + 1) * T_SC]), "ve")

                    M[("veu",) + nk] = sm.val("ve")
                    ve_u = sm.val("ve")

                    for n in range(N_ST):
                        nb = n % 2
                        j = g * 16 + n

                        @blk.scalar
                        def _(s, c=c, gb=gb, n=n, nb=nb, j=j):
                            if (c, j) > (0, 1):
                                s.wait_ge(sm.h["ve"], M[("vesc", c, j)])
                            sm.inc(s.activation(out=dA[nb][:], in_=delta[gb][:],
                                                func=AF.Exp, scale=a_neg[:, j:j + 1]),
                                   "sc")

                        sc_dA = sm.val("sc")

                        @blk.gpsimd
                        def _(gp, c=c, gb=gb, n=n, nb=nb, j=j):
                            gp.wait_ge(sm.h["ve"], ve_u)
                            gp.wait_ge(sm.h["bc"], M[("bcr", c)])
                            if (c, j) > (0, 1):
                                gp.wait_ge(sm.h["ve"], M[("vesc", c, j)])
                            sm.inc(gp.tensor_mul(dBx[nb][:], u_g[gb][:],
                                                 brep[:, n * T_SC:(n + 1) * T_SC]), "gp")

                        gp_dBx = sm.val("gp")

                        @blk.vector
                        def _(v, c=c, n=n, nb=nb, j=j):
                            v.wait_ge(sm.h["sc"], sc_dA)
                            v.wait_ge(sm.h["gp"], gp_dBx)
                            init = 0.0 if c == 0 else carry[:, j:j + 1]
                            v.tensor_tensor_scan(out=h_b[nb][:], data0=dA[nb][:],
                                                 data1=dBx[nb][:], initial=init,
                                                 op0=ALU.mult, op1=ALU.add)
                            v.tensor_mul(hc[:], h_b[nb][:],
                                         crep[:, n * T_SC:(n + 1) * T_SC])
                            if n == 0:
                                v.wait_ge(sm.h["sc"], sc_yinit)
                            sm.inc(v.tensor_add(y_g[gb][:], y_g[gb][:], hc[:]), "ve")

                        ve_scan = sm.val("ve")
                        M[("vesc", c, j + 2) if j + 2 < G * 16 else ("vesc", c + 1, j + 2 - G * 16)] = ve_scan

                        @blk.gpsimd
                        def _(gp, j=j, nb=nb):
                            gp.wait_ge(sm.h["ve"], ve_scan)
                            sm.inc(gp.tensor_copy(carry[:, j:j + 1],
                                                  h_b[nb][:, T_SC - 1:T_SC]), "gp")

                    ve_all_n = sm.val("ve")

                    @blk.sync
                    def _(s, c=c, g=g, gb=gb):
                        s.wait_ge(sm.h["ve"], ve_all_n)
                        sm.inc(s.dma_start(
                            out=y_out[g * 128:(g + 1) * 128, c * T_SC:(c + 1) * T_SC],
                            in_=y_g[gb][:]), "st", 16)

                    M[("sty",) + nk] = sm.val("st")

                M[("vend", c)] = sm.val("ve")
                M[("scend", c)] = sm.val("sc")
                M[("gpend", c)] = sm.val("gp")

            @blk.sync
            def _(s):
                s.wait_ge(sm.h["st"], sm.val("st"))

    return nc


def scan_inputs(inputs, c):
    b, p = c // 2, c % 2
    x = np.asarray(inputs["L0_attn_x"][b])
    xT = np.ascontiguousarray(x.T)
    dh = slice(p * 1024, (p + 1) * 1024)
    xTh = np.ascontiguousarray(xT[dh])
    xprojT = np.ascontiguousarray(np.asarray(inputs["L0_attn_x_proj_w"]).T)
    dtwT = np.ascontiguousarray(np.asarray(inputs["L0_attn_dt_proj_w"]).T[:, dh])
    a_log = np.ascontiguousarray(
        np.asarray(inputs["L0_attn_A_log"])[dh].reshape(8, 128, 16)
        .transpose(1, 0, 2).reshape(128, 8 * 16))
    d_prm = np.ascontiguousarray(
        np.asarray(inputs["L0_attn_D_param"])[dh].reshape(8, 128).T)
    b_dt = np.ascontiguousarray(
        np.asarray(inputs["L0_attn_dt_proj_b"])[dh].reshape(8, 128).T)
    return dict(xT=xT, xTh=xTh, xprojT=xprojT, dtwT=dtwT,
                a_log=a_log, d_prm=d_prm, b_dt=b_dt)


def run_scan_phase(inputs):
    nc = build_scan(G=8)
    in_maps = [scan_inputs(inputs, c) for c in range(8)]
    res = run_bass_kernel_spmd(nc, in_maps, core_ids=list(range(8)))
    y = np.empty((4, L_SEQ, D_MODEL), np.float32)
    for c in range(8):
        b, p = c // 2, c % 2
        y[b, :, p * 1024:(p + 1) * 1024] = np.asarray(res.results[c]["y_out"]).T
    return y





TOK = 2048
TC = 512
N_TC = TOK // TC
SCALE_ATT = 0.08838834764831845

def build_blocks(dbg=False):
    nc = bass.Bass("TRN2", target_bir_lowering=False, dynamic_dma_scratch_size=8192)

    ein = lambda n, sh, dt: nc.dram_tensor(n, sh, dt, kind="ExternalInput").ap()
    x_res = ein("x_res", [D_MODEL, TOK], F32)
    y_own = ein("y_own", [D_MODEL, TOK], F32)
    wg_u8 = ein("wg_u8", [D_MODEL, 8192], U8)
    wu_u8 = ein("wu_u8", [D_MODEL, 8192], U8)
    wd_u8 = ein("wd_u8", [8192, D_MODEL], U8)
    sg = ein("sg", [128, 64], F32)
    su = ein("su", [128, 64], F32)
    sd = ein("sd", [128, 16], F32)
    wqT = ein("wqT", [D_MODEL, D_MODEL], F32)
    woT = ein("woT", [D_MODEL, D_MODEL], F32)
    kT = ein("kT", [D_MODEL, 4096], F32)
    v_in = ein("v_in", [4096, D_MODEL], F32)
    g1_u8 = ein("g1_u8", [D_MODEL, 5632], U8)
    u1_u8 = ein("u1_u8", [D_MODEL, 5632], U8)
    d1_u8 = ein("d1_u8", [5632, D_MODEL], U8)
    s1g = ein("s1g", [128, 44], F32)
    s1u = ein("s1u", [128, 44], F32)
    s1d = ein("s1d", [128, 16], F32)

    outT = nc.dram_tensor("outT", [D_MODEL, TOK], F32, kind="ExternalOutput").ap()
    dbg_out = (nc.dram_tensor("dbg_out", [8, D_MODEL, TC], F32, kind="ExternalOutput").ap()
               if dbg else None)

    dint = lambda n, sh, dt: nc.dram_tensor(n, sh, dt, kind="Internal").ap()
    wgB = dint("wgB", [D_MODEL, 8192], BF16)
    wuB = dint("wuB", [D_MODEL, 8192], BF16)
    wdB = dint("wdB", [8192, D_MODEL], BF16)
    g1B = dint("g1B", [D_MODEL, 5632], BF16)
    u1B = dint("u1B", [D_MODEL, 5632], BF16)
    d1B = dint("d1B", [5632, D_MODEL], BF16)
    wqB = dint("wqB", [D_MODEL, D_MODEL], BF16)
    woB = dint("woB", [D_MODEL, D_MODEL], BF16)
    kTB = dint("kTB", [D_MODEL, 4096], BF16)
    vB = dint("vB", [4096, D_MODEL], BF16)

    es = ExitStack()
    with es:
        sb = lambda n, sh, dt: es.enter_context(nc.sbuf_tensor(n, sh, dt))
        ps = lambda n, sh, dt: es.enter_context(nc.psum_tensor(n, sh, dt))

        r_s = sb("r_s", [128, 16 * TC], F32)
        lnf = sb("lnf", [128, 16 * TC], F32)
        lnb = sb("lnb", [128, 16 * TC], BF16)
        arena = sb("arena", [128, 64 * TC], BF16)
        ar32 = arena[:].bitcast(F32)
        wgu = [sb(f"wgu{i}", [128, 16 * 128], BF16) for i in range(4)]
        wdn = [sb(f"wdn{i}", [128, 64 * 128], BF16) for i in range(1)]
        kbl = [sb(f"kbl{i}", [128, 4096], BF16) for i in range(1)]
        vbl = [sb(f"vbl{i}", [128, 32 * 128], BF16) for i in range(1)]
        gsil = [sb(f"gsil{i}", [128, TC], BF16) for i in range(2)]
        usil = [sb(f"usil{i}", [128, TC], BF16) for i in range(2)]
        recb = sb("recb", [128, TC], F32)
        sg_s = sb("sg_s", [128, 64], F32)
        su_s = sb("su_s", [128, 64], F32)
        sd_s = sb("sd_s", [128, 16], F32)
        s1g_s = sb("s1g_s", [128, 44], F32)
        s1u_s = sb("s1u_s", [128, 44], F32)
        s1d_s = sb("s1d_s", [128, 16], F32)
        onesc_b = sb("onesc_b", [128, 1], BF16)
        onesc_f = sb("onesc_f", [128, 1], F32)
        onesr_f = sb("onesr_f", [1, 128], F32)
        ones128 = sb("ones128", [128, 128], BF16)
        m128r = sb("m128r", [1, 128], BF16)
        csb = sb("csb", [1, TC], BF16)
        csb2 = sb("csb2", [1, TC], BF16)
        stt1 = sb("stt1", [1, TC], F32)
        stt2 = sb("stt2", [1, TC], F32)
        stt3 = sb("stt3", [1, TC], F32)
        stt4 = sb("stt4", [1, TC], F32)

        pA = ps("pA", [128, TC], F32)
        pB = ps("pB", [128, TC], F32)
        pC = ps("pC", [128, TC], F32)
        pD = ps("pD", [128, TC], F32)
        pE = ps("pE", [128, TC], F32)
        pDen = ps("pDen", [128, TC], F32)
        pS1 = ps("pS1", [1, TC], F32)
        pS2 = ps("pS2", [1, TC], F32)

        sm = Sems(es, nc, ["ld", "mm", "sc", "ve", "gp", "st"])
        M = {}
        consts_emit = setup_consts(nc, es)

        with nc.Block() as blk:
            @blk.vector
            def _(v):
                consts_emit(v)
                v.memset(onesc_b[:], 1.0)
                v.memset(onesc_f[:], 1.0)
                v.memset(onesr_f[:], 1.0)
                v.memset(ones128[:], 1.0)
                sm.inc(v.memset(m128r[:], -128.0), "ve")

            prep_ve = sm.val("ve")

            @blk.gpsimd
            def _(g):
                for src, dst in ((wg_u8, wgB), (wu_u8, wuB), (wd_u8, wdB),
                                 (wqT, wqB), (kT, kTB), (v_in, vB), (woT, woB),
                                 (g1_u8, g1B), (u1_u8, u1B), (d1_u8, d1B)):
                    sm.inc(g.dma_start(out=dst, in_=src), "gp", 16)

            prep_gp = sm.val("gp")

            @blk.sync
            def _(s):
                for src, dst in ((sg, sg_s), (su, su_s), (sd, sd_s),
                                 (s1g, s1g_s), (s1u, s1u_s), (s1d, s1d_s)):
                    sm.inc(s.dma_start(out=dst[:], in_=src), "ld", 16)

            prep_ld = sm.val("ld")

            def emit_ln(tag, tc, last_mm, last_ve, last_sc, final=False):
                @blk.scalar
                def _(s):
                    s.wait_ge(sm.h["ve"], last_ve)
                    sm.inc(s.activation(out=ar32[:, 0:16 * TC], in_=r_s[:],
                                        func=AF.Square), "sc")
                sq_sc = sm.val("sc")

                @blk.tensor
                def _(t):
                    t.wait_ge(sm.h["ve"], last_ve)
                    t.wait_ge(sm.h["mm"], last_mm)
                    for k in range(16):
                        t.matmul(pS1[:], onesc_f[:], r_s[:, k * TC:(k + 1) * TC],
                                 start=(k == 0), stop=(k == 15))
                    t.wait_ge(sm.h["sc"], sq_sc)
                    for k in range(16):
                        ins = t.matmul(pS2[:], onesc_f[:],
                                       ar32[:, k * TC:(k + 1) * TC],
                                       start=(k == 0), stop=(k == 15))
                    sm.inc(ins, "mm")
                st_mm = sm.val("mm")

                @blk.vector
                def _(v):
                    v.wait_ge(sm.h["mm"], st_mm)
                    v.tensor_scalar_mul(stt1[:], pS1[:], 1.0 / 2048.0)
                    v.tensor_scalar_mul(stt4[:], pS2[:], 1.0 / 2048.0)
                    v.tensor_mul(stt2[:], stt1[:], stt1[:])
                    sm.inc(v.tensor_sub(stt4[:], stt4[:], stt2[:]), "ve")

                var_ve = sm.val("ve")

                @blk.scalar
                def _(s):
                    s.wait_ge(sm.h["ve"], var_ve)
                    s.activation(out=stt2[:], in_=stt4[:], func=AF.Ln,
                                 bias=1e-5, scale=1.0)
                    sm.inc(s.activation(out=stt2[:], in_=stt2[:], func=AF.Exp,
                                        scale=-0.5), "sc")

                rstd_sc = sm.val("sc")

                @blk.vector
                def _(v):
                    v.wait_ge(sm.h["sc"], rstd_sc)
                    v.tensor_mul(stt3[:], stt1[:], stt2[:])
                    sm.inc(v.tensor_scalar_mul(stt3[:], stt3[:], -1.0), "ve")

                m1_ve = sm.val("ve")

                @blk.tensor
                def _(t):
                    t.wait_ge(sm.h["ve"], m1_ve)
                    t.matmul(pA[:], onesr_f[:], stt2[:], start=True, stop=True)
                    sm.inc(t.matmul(pB[:], onesr_f[:], stt3[:],
                                    start=True, stop=True), "mm")

                bc_mm = sm.val("mm")

                @blk.vector
                def _(v):
                    v.wait_ge(sm.h["mm"], bc_mm)
                    bA = bass.AP(pA.tensor if hasattr(pA, "tensor") else pA._handle, 0, []) if False else None
                    v.tensor_tensor(
                        out=lnf[:].rearrange("p (a t) -> p a t", t=TC),
                        in0=r_s[:].rearrange("p (a t) -> p a t", t=TC),
                        in1=pA[:].unsqueeze(1).broadcast_to([128, 16, TC]),
                        op=ALU.mult)
                    sm.inc(v.tensor_tensor(
                        out=lnf[:].rearrange("p (a t) -> p a t", t=TC),
                        in0=lnf[:].rearrange("p (a t) -> p a t", t=TC),
                        in1=pB[:].unsqueeze(1).broadcast_to([128, 16, TC]),
                        op=ALU.add), "ve")

                ln_ve = sm.val("ve")
                if final:
                    return dict(mm=sm.val("mm"), ve=ln_ve, sc=sm.val("sc"))

                @blk.scalar
                def _(s):
                    s.wait_ge(sm.h["ve"], ln_ve)
                    sm.inc(s.copy(lnb[:], lnf[:]), "sc")
                lnb_sc = sm.val("sc")

                @blk.tensor
                def _(t):
                    t.wait_ge(sm.h["sc"], lnb_sc)
                    for k in range(16):
                        ins = t.matmul(pS1[:], onesc_b[:], lnb[:, k * TC:(k + 1) * TC],
                                       start=(k == 0), stop=(k == 15))
                    sm.inc(ins, "mm")
                cs_mm = sm.val("mm")

                @blk.scalar
                def _(s):
                    s.wait_ge(sm.h["mm"], cs_mm)
                    sm.inc(s.copy(csb[:], pS1[:]), "sc")

                return dict(mm=sm.val("mm"), ve=ln_ve, sc=sm.val("sc"),
                            lnb_sc=lnb_sc, cs_sc=sm.val("sc"))

            def emit_ffn(tag, tc, n_it, wgt, wut, wdt, sgA, suA, sdA, lnm):
                pg = [pA, pC]
                pu = [pB, pD]
                mm_git = {}
                sc_git = {}
                ve_hit = {}
                for it in range(n_it):
                    ib = it % 2

                    mm_snap = sm.val("mm")

                    @blk.sync
                    def _(s, it=it, ib=ib, mm_snap=mm_snap):
                        s.wait_ge(sm.h["gp"], prep_gp)
                        if it >= 2:
                            s.wait_ge(sm.h["mm"], mm_git[it - 2])
                        else:
                            s.wait_ge(sm.h["mm"], mm_snap)
                        sm.inc(s.dma_start(
                            out=wgu[ib][:].rearrange("p (a m) -> p a m", m=128),
                            in_=wgt.rearrange("(a p) i -> p a i", p=128)[:, :, it * 128:(it + 1) * 128]),
                            "ld", 16)
                        sm.inc(s.dma_start(
                            out=wgu[2 + ib][:].rearrange("p (a m) -> p a m", m=128),
                            in_=wut.rearrange("(a p) i -> p a i", p=128)[:, :, it * 128:(it + 1) * 128]),
                            "ld", 16)

                    ld_it = sm.val("ld")

                    @blk.tensor
                    def _(t, it=it, ib=ib):
                        t.wait_ge(sm.h["ld"], ld_it)
                        if it == 0:
                            t.wait_ge(sm.h["sc"], lnm["cs_sc"])
                        if it >= 2:
                            t.wait_ge(sm.h["sc"], sc_git[it - 2])
                        for k in range(16):
                            t.matmul(pg[ib][:], wgu[ib][:, k * 128:(k + 1) * 128],
                                     lnb[:, k * TC:(k + 1) * TC],
                                     start=(k == 0), stop=False)
                        t.matmul(pg[ib][:], m128r[:], csb[:], start=False, stop=True)
                        for k in range(16):
                            t.matmul(pu[ib][:], wgu[2 + ib][:, k * 128:(k + 1) * 128],
                                     lnb[:, k * TC:(k + 1) * TC],
                                     start=(k == 0), stop=False)
                        sm.inc(t.matmul(pu[ib][:], m128r[:], csb[:],
                                        start=False, stop=True), "mm")

                    mm_git[it] = sm.val("mm")

                    ve_snap = sm.val("ve")

                    @blk.scalar
                    def _(s, it=it, ib=ib, ve_snap=ve_snap):
                        s.wait_ge(sm.h["mm"], mm_git[it])
                        if it >= 2:
                            s.wait_ge(sm.h["ve"], ve_hit[it - 2])
                        else:
                            s.wait_ge(sm.h["ve"], ve_snap)
                        s.activation(out=gsil[ib][:], in_=pg[ib][:], func=AF.Silu,
                                     scale=sgA[:, it:it + 1])
                        sm.inc(s.activation(out=usil[ib][:], in_=pu[ib][:],
                                            func=AF.Copy, scale=suA[:, it:it + 1]), "sc")

                    sc_git[it] = sm.val("sc")

                    @blk.vector
                    def _(v, it=it, ib=ib):
                        v.wait_ge(sm.h["sc"], sc_git[it])
                        sm.inc(v.tensor_mul(arena[:, it * TC:(it + 1) * TC],
                                            gsil[ib][:], usil[ib][:]), "ve")

                    ve_hit[it] = sm.val("ve")

                    @blk.tensor
                    def _(t, it=it):
                        t.wait_ge(sm.h["ve"], ve_hit[it])
                        ins = t.matmul(pS1[:], onesc_b[:],
                                       arena[:, it * TC:(it + 1) * TC],
                                       start=(it == 0), stop=(it == n_it - 1))
                        if it == n_it - 1:
                            sm.inc(ins, "mm")

                csh_mm = sm.val("mm")

                @blk.scalar
                def _(s):
                    s.wait_ge(sm.h["mm"], csh_mm)
                    sm.inc(s.copy(csb2[:], pS1[:]), "sc")

                csh_sc = sm.val("sc")

                pdn = [pA, pC]
                mm_dot = {}
                ve_dot = {}
                for ot in range(16):
                    ob = 0

                    mm_snap2 = sm.val("mm")

                    @blk.sync
                    def _(s, ot=ot, ob=ob, mm_snap2=mm_snap2):
                        if ot >= 1:
                            s.wait_ge(sm.h["mm"], mm_dot[ot - 1])
                        else:
                            s.wait_ge(sm.h["mm"], mm_snap2)
                        sm.inc(s.dma_start(
                            out=wdn[ob][:, 0:n_it * 128].rearrange("p (a m) -> p a m", m=128),
                            in_=wdt.rearrange("(a p) m -> p a m", p=128)[:, 0:n_it, ot * 128:(ot + 1) * 128]),
                            "ld", 16)

                    ld_ot = sm.val("ld")

                    @blk.tensor
                    def _(t, ot=ot):
                        pb = ot % 2
                        t.wait_ge(sm.h["ld"], ld_ot)
                        t.wait_ge(sm.h["sc"], csh_sc)
                        if ot >= 2:
                            t.wait_ge(sm.h["ve"], ve_dot[ot - 2])
                        for k in range(n_it):
                            t.matmul(pdn[pb][:], wdn[0][:, k * 128:(k + 1) * 128],
                                     arena[:, k * TC:(k + 1) * TC],
                                     start=(k == 0), stop=False)
                        sm.inc(t.matmul(pdn[pb][:], m128r[:], csb2[:],
                                        start=False, stop=True), "mm")

                    mm_dot[ot] = sm.val("mm")

                    @blk.vector
                    def _(v, ot=ot):
                        pb = ot % 2
                        v.wait_ge(sm.h["mm"], mm_dot[ot])
                        ins = v.scalar_tensor_tensor(
                            out=r_s[:, ot * TC:(ot + 1) * TC],
                            in0=pdn[pb][:], scalar=sdA[:, ot:ot + 1],
                            in1=lnf[:, ot * TC:(ot + 1) * TC],
                            op0=ALU.mult, op1=ALU.add)
                        sm.inc(ins, "ve")

                    ve_dot[ot] = sm.val("ve")

                return dict(mm=sm.val("mm"), ve=sm.val("ve"), sc=sm.val("sc"))

            def emit_attn(tag, tc, lnm):
                qt = arena[:, 0:16 * TC]
                otn = arena[:, 16 * TC:32 * TC]
                expT = arena[:, 32 * TC:64 * TC]
                pq = [pA, pC]
                mm_qot = {}
                sc_qot = {}
                for ot in range(16):
                    ob = ot % 2

                    mm_snap = sm.val("mm")

                    @blk.sync
                    def _(s, ot=ot, ob=ob, mm_snap=mm_snap):
                        s.wait_ge(sm.h["gp"], prep_gp)
                        if ot >= 2:
                            s.wait_ge(sm.h["mm"], mm_qot[ot - 2])
                        else:
                            s.wait_ge(sm.h["mm"], mm_snap)
                        sm.inc(s.dma_start(
                            out=wgu[ob][:].rearrange("p (a m) -> p a m", m=128),
                            in_=wqB.rearrange("(a p) m -> p a m", p=128)[:, :, ot * 128:(ot + 1) * 128]),
                            "ld", 16)

                    ld_ot = sm.val("ld")

                    @blk.tensor
                    def _(t, ot=ot, ob=ob):
                        t.wait_ge(sm.h["ld"], ld_ot)
                        if ot == 0:
                            t.wait_ge(sm.h["sc"], lnm["lnb_sc"])
                            t.wait_ge(sm.h["mm"], lnm["mm"])
                        if ot >= 2:
                            t.wait_ge(sm.h["sc"], sc_qot[ot - 2])
                        for k in range(16):
                            ins = t.matmul(pq[ob][:], wgu[ob][:, k * 128:(k + 1) * 128],
                                           lnb[:, k * TC:(k + 1) * TC],
                                           start=(k == 0), stop=(k == 15))
                        sm.inc(ins, "mm")

                    mm_qot[ot] = sm.val("mm")

                    @blk.scalar
                    def _(s, ot=ot, ob=ob):
                        s.wait_ge(sm.h["mm"], mm_qot[ot])
                        sm.inc(s.copy(qt[:, ot * TC:(ot + 1) * TC], pq[ob][:]), "sc")

                    sc_qot[ot] = sm.val("sc")

                qt_sc = sm.val("sc")

                mm_sco = {}
                sc_exp = {}
                ve_hd = {}
                mm_den = {}
                mm_av = {}
                for hd in range(16):
                    hb = 0

                    mm_snapk = sm.val("mm")

                    @blk.sync
                    def _(s, hd=hd, hb=hb, mm_snapk=mm_snapk):
                        if hd >= 1:
                            s.wait_ge(sm.h["mm"], mm_av[hd - 1])
                        else:
                            s.wait_ge(sm.h["mm"], mm_snapk)
                        sm.inc(s.dma_start(out=kbl[hb][:],
                                    in_=kTB[hd * 128:(hd + 1) * 128, :]), "ld", 16)
                        sm.inc(s.dma_start(
                            out=vbl[hb][:].rearrange("p (a m) -> p a m", m=128),
                            in_=vB.rearrange("(a p) m -> p a m", p=128)[:, :, hd * 128:(hd + 1) * 128]),
                            "ld", 16)

                    ld_hd = sm.val("ld")

                    psc = [pB, pD]
                    for kvt in range(32):
                        kb = kvt % 2

                        @blk.tensor
                        def _(t, hd=hd, hb=hb, kvt=kvt, kb=kb):
                            t.wait_ge(sm.h["ld"], ld_hd)
                            if hd == 0 and kvt == 0:
                                t.wait_ge(sm.h["sc"], qt_sc)
                            if (hd, kvt) > (0, 1):
                                t.wait_ge(sm.h["sc"], M[("exp", tag, hd, kvt)])
                            ins = t.matmul(psc[kb][:],
                                           kbl[hb][:, kvt * 128:(kvt + 1) * 128],
                                           qt[:, hd * TC:(hd + 1) * TC],
                                           start=True, stop=True)
                            sm.inc(ins, "mm")

                        mm_s = sm.val("mm")

                        @blk.scalar
                        def _(s, hd=hd, kvt=kvt, kb=kb):
                            s.wait_ge(sm.h["mm"], mm_s)
                            if hd >= 2 and kvt < 2:
                                s.wait_ge(sm.h["mm"], mm_av[hd - 2])
                            ins = s.activation(out=expT[:, kvt * TC:(kvt + 1) * TC],
                                               in_=psc[kb][:], func=AF.Exp,
                                               scale=SCALE_ATT, bias=-20.0)
                            sm.inc(ins, "sc")

                        nxt = (hd, kvt + 2) if kvt + 2 < 32 else (hd + 1, kvt + 2 - 32)
                        M[("exp", tag) + nxt] = sm.val("sc")
                        sc_exp[(hd, kvt)] = sm.val("sc")

                    exp_all = sm.val("sc")

                    @blk.tensor
                    def _(t, hd=hd, hb=hb):
                        t.wait_ge(sm.h["sc"], exp_all)
                        if hd >= 1:
                            t.wait_ge(sm.h["sc"], M[("rec", tag, hd - 1)])
                            t.wait_ge(sm.h["ve"], M[("otn", tag, hd - 1)])
                        for kvt in range(32):
                            t.matmul(pDen[:], ones128[:],
                                     expT[:, kvt * TC:(kvt + 1) * TC],
                                     start=(kvt == 0), stop=(kvt == 31))
                        sm.inc(t.nop() if False else t.matmul(
                            pDen[:], ones128[0:1, 0:1].broadcast_to([1, 1]), csb[0:1, 0:1],
                            start=False, stop=True), "mm") if False else None
                        for kvt in range(32):
                            ins = t.matmul(pE[:], vbl[hb][:, kvt * 128:(kvt + 1) * 128],
                                           expT[:, kvt * TC:(kvt + 1) * TC],
                                           start=(kvt == 0), stop=(kvt == 31))
                        sm.inc(ins, "mm")

                    mm_den[hd] = sm.val("mm")
                    mm_av[hd] = sm.val("mm")

                    @blk.scalar
                    def _(s, hd=hd, tag=tag):
                        s.wait_ge(sm.h["mm"], mm_av[hd])
                        if hd >= 1:
                            s.wait_ge(sm.h["ve"], M[("otn", tag, hd - 1)])
                        s.activation(out=recb[:], in_=pDen[:], func=AF.Ln)
                        sm.inc(s.activation(out=recb[:], in_=recb[:], func=AF.Exp,
                                            scale=-1.0), "sc")

                    M[("rec", tag, hd)] = sm.val("sc")
                    rec_sc = sm.val("sc")

                    @blk.vector
                    def _(v, hd=hd):
                        v.wait_ge(sm.h["sc"], rec_sc)
                        sm.inc(v.tensor_mul(otn[:, hd * TC:(hd + 1) * TC],
                                            pE[:], recb[:]), "ve")

                    M[("otn", tag, hd)] = sm.val("ve")
                    ve_hd[hd] = sm.val("ve")

                otn_ve = sm.val("ve")

                pw = [pA, pC]
                mm_wot = {}
                ve_wot = {}
                for ot in range(16):
                    ob = ot % 2

                    mm_snap = sm.val("mm")

                    @blk.sync
                    def _(s, ot=ot, ob=ob, mm_snap=mm_snap):
                        if ot >= 2:
                            s.wait_ge(sm.h["mm"], mm_wot[ot - 2])
                        else:
                            s.wait_ge(sm.h["mm"], mm_snap)
                        sm.inc(s.dma_start(
                            out=wgu[ob][:].rearrange("p (a m) -> p a m", m=128),
                            in_=woB.rearrange("(a p) m -> p a m", p=128)[:, :, ot * 128:(ot + 1) * 128]),
                            "ld", 16)

                    ld_ot = sm.val("ld")

                    @blk.tensor
                    def _(t, ot=ot, ob=ob):
                        t.wait_ge(sm.h["ld"], ld_ot)
                        t.wait_ge(sm.h["ve"], otn_ve)
                        if ot >= 2:
                            t.wait_ge(sm.h["ve"], ve_wot[ot - 2])
                        for k in range(16):
                            ins = t.matmul(pw[ob][:], wgu[ob][:, k * 128:(k + 1) * 128],
                                           otn[:, k * TC:(k + 1) * TC],
                                           start=(k == 0), stop=(k == 15))
                        sm.inc(ins, "mm")

                    mm_wot[ot] = sm.val("mm")

                    @blk.vector
                    def _(v, ot=ot, ob=ob):
                        v.wait_ge(sm.h["mm"], mm_wot[ot])
                        sm.inc(v.tensor_add(r_s[:, ot * TC:(ot + 1) * TC],
                                            pw[ob][:],
                                            lnf[:, ot * TC:(ot + 1) * TC]), "ve")

                    ve_wot[ot] = sm.val("ve")

                return dict(mm=sm.val("mm"), ve=sm.val("ve"), sc=sm.val("sc"))

            for tc in range(N_TC):
                @blk.sync
                def _(s, tc=tc):
                    if tc >= 1:
                        s.wait_ge(sm.h["st"], M[("out", tc - 1)])
                        s.wait_ge(sm.h["ve"], M[("rdone", tc - 1)])
                    sm.inc(s.dma_start(
                        out=lnf[:].rearrange("p (a t) -> p a t", t=TC),
                        in_=x_res.rearrange("(a p) t -> p a t", p=128)[:, :, tc * TC:(tc + 1) * TC]),
                        "ld", 16)
                    sm.inc(s.dma_start(
                        out=r_s[:].rearrange("p (a t) -> p a t", t=TC),
                        in_=y_own.rearrange("(a p) t -> p a t", p=128)[:, :, tc * TC:(tc + 1) * TC]),
                        "ld", 16)

                ld_xy = sm.val("ld")

                @blk.vector
                def _(v, tc=tc):
                    v.wait_ge(sm.h["ld"], ld_xy)
                    if tc == 0:
                        v.wait_ge(sm.h["ve"], prep_ve)
                        v.wait_ge(sm.h["ld"], prep_ld)
                    sm.inc(v.tensor_add(r_s[:], r_s[:], lnf[:]), "ve")

                r0_ve = sm.val("ve")

                def snap(idx, buf, vewait, tc=tc):
                    if not dbg or tc != 0:
                        return
                    @blk.sync
                    def _(s):
                        s.wait_ge(sm.h["ve"], vewait)
                        sm.inc(s.dma_start(
                            out=dbg_out[idx].rearrange("(a p) t -> p a t", p=128),
                            in_=buf[:].rearrange("p (a t) -> p a t", t=TC)), "st", 16)

                snap(0, r_s, r0_ve)
                ln0 = emit_ln("ln0", tc, sm.val("mm"), r0_ve, sm.val("sc"))
                snap(1, lnf, ln0["ve"])
                f0 = emit_ffn("ffn0", tc, 64, wgB, wuB, wdB, sg_s, su_s, sd_s, ln0)
                snap(2, r_s, f0["ve"])
                ln1 = emit_ln("ln1", tc, f0["mm"], f0["ve"], f0["sc"])
                snap(3, lnf, ln1["ve"])
                at = emit_attn(f"at{tc}", tc, ln1)
                snap(4, r_s, at["ve"])
                ln2 = emit_ln("ln2", tc, at["mm"], at["ve"], at["sc"])
                snap(5, lnf, ln2["ve"])
                f1 = emit_ffn("ffn1", tc, 44, g1B, u1B, d1B, s1g_s, s1u_s, s1d_s, ln2)
                snap(6, r_s, f1["ve"])
                ln3 = emit_ln("ln3", tc, f1["mm"], f1["ve"], f1["sc"], final=True)
                snap(7, lnf, ln3["ve"])
                M[("rdone", tc)] = sm.val("ve")

                @blk.sync
                def _(s, tc=tc):
                    s.wait_ge(sm.h["ve"], ln3["ve"])
                    sm.inc(s.dma_start(
                        out=outT.rearrange("(a p) t -> p a t", p=128)[:, :, tc * TC:(tc + 1) * TC],
                        in_=lnf[:].rearrange("p (a t) -> p a t", t=TC)), "st", 16)

                M[("out", tc)] = sm.val("st")

            @blk.sync
            def _(s):
                s.wait_ge(sm.h["st"], sm.val("st"))

    return nc


def _u8(w_int32_T):
    a = np.ascontiguousarray(w_int32_T)
    return np.ascontiguousarray(a.view(np.uint8).reshape(a.shape[0], a.shape[1], 4)[:, :, 0])


def blocks_inputs(inputs, c, y):
    b, p = c // 2, c % 2
    tok = slice(p * TOK, (p + 1) * TOK)
    x = np.asarray(inputs["L0_attn_x"][b])
    xT = np.ascontiguousarray(x.T[:, tok])
    yT = np.ascontiguousarray(y[b].T[:, tok])
    d = {}
    d["x_res"] = xT
    d["y_own"] = yT
    d["wg_u8"] = _u8(np.asarray(inputs["L0_ffn_gate_w"]).T)
    d["wu_u8"] = _u8(np.asarray(inputs["L0_ffn_up_w"]).T)
    d["wd_u8"] = _u8(np.asarray(inputs["L0_ffn_down_w"]).T)
    d["sg"] = np.ascontiguousarray(np.asarray(inputs["L0_ffn_gate_scale"]).reshape(64, 128).T)
    d["su"] = np.ascontiguousarray(np.asarray(inputs["L0_ffn_up_scale"]).reshape(64, 128).T)
    d["sd"] = np.ascontiguousarray(np.asarray(inputs["L0_ffn_down_scale"]).reshape(16, 128).T)
    d["wqT"] = np.ascontiguousarray(np.asarray(inputs["L1_attn_Wq"]).T)
    d["woT"] = np.ascontiguousarray(np.asarray(inputs["L1_attn_Wo"]).T)
    d["kT"] = np.ascontiguousarray(np.asarray(inputs["L1_attn_kv_k"]).reshape(4096, 2048).T)
    d["v_in"] = np.ascontiguousarray(np.asarray(inputs["L1_attn_kv_v"]).reshape(4096, 2048))
    d["g1_u8"] = _u8(np.asarray(inputs["L1_ffn_wq_gate"]).T)
    d["u1_u8"] = _u8(np.asarray(inputs["L1_ffn_wq_up"]).T)
    d["d1_u8"] = _u8(np.asarray(inputs["L1_ffn_wq_down"]).T)
    d["s1g"] = np.ascontiguousarray(np.asarray(inputs["L1_ffn_s_gate"]).reshape(44, 128).T)
    d["s1u"] = np.ascontiguousarray(np.asarray(inputs["L1_ffn_s_up"]).reshape(44, 128).T)
    d["s1d"] = np.ascontiguousarray(np.asarray(inputs["L1_ffn_s_down"]).reshape(16, 128).T)
    return d


def kernel(**inputs):
    y = run_scan_phase(inputs)
    nc = build_blocks()
    in_maps = [blocks_inputs(inputs, c, y) for c in range(8)]
    res = run_bass_kernel_spmd(nc, in_maps, core_ids=list(range(8)))
    out = np.empty((4, L_SEQ, D_MODEL), np.float32)
    for c in range(8):
        b, p = c // 2, c % 2
        out[b, p * TOK:(p + 1) * TOK, :] = np.asarray(res.results[c]["outT"]).T
    return out


# version 04
# speedup vs baseline: 1.5632x; 1.5632x over previous
import numpy as np
from contextlib import ExitStack

import concourse.bass as bass
import concourse.mybir as mybir
from concourse.bass_utils import run_bass_kernel_spmd

F32 = mybir.dt.float32
BF16 = mybir.dt.bfloat16
U8 = mybir.dt.uint8
AF = mybir.ActivationFunctionType
ALU = mybir.AluOpType
AX = mybir.AxisListType

D_MODEL = 2048
L_SEQ = 4096
N_ST = 16
T_SC = 512
N_CHUNK = L_SEQ // T_SC

CONSTS = [0.0, 1.0, 0.5, -0.5, 2.0, -1.0, 1e-5, -20.0,
          0.08838834764831845, 1.0 / 2048.0, -128.0, -2.0]


class Sems:

    def __init__(self, es, nc, names):
        self.h = {}
        self.c = {}
        for n in names:
            self.h[n] = es.enter_context(nc.semaphore(n))
            self.c[n] = 0

    def inc(self, ins, name, v=1):
        ins.then_inc(self.h[name], v)
        self.c[name] += v
        return self.c[name]

    def val(self, name):
        return self.c[name]


def setup_consts(nc, es):
    ct = es.enter_context(nc.sbuf_tensor("consts", [128, len(CONSTS)], F32))
    for j, val in enumerate(CONSTS):
        nc.const_aps.register(F32, val, ct[:, j:j + 1])

    def emit(v):
        for j, val in enumerate(CONSTS):
            v.memset(ct[:, j:j + 1], val)
    return emit



def build_scan(G=8):
    nc = bass.Bass("TRN2", target_bir_lowering=False)
    DH = G * 128

    xT = nc.dram_tensor("xT", [D_MODEL, L_SEQ], F32, kind="ExternalInput").ap()
    xTh = nc.dram_tensor("xTh", [DH, L_SEQ], F32, kind="ExternalInput").ap()
    xprojT = nc.dram_tensor("xprojT", [D_MODEL, 160], F32, kind="ExternalInput").ap()
    dtwT = nc.dram_tensor("dtwT", [128, DH], F32, kind="ExternalInput").ap()
    a_log = nc.dram_tensor("a_log", [128, G * 16], F32, kind="ExternalInput").ap()
    d_prm = nc.dram_tensor("d_prm", [128, G], F32, kind="ExternalInput").ap()
    b_dt = nc.dram_tensor("b_dt", [128, G], F32, kind="ExternalInput").ap()

    y_out = nc.dram_tensor("y_out", [DH, L_SEQ], F32, kind="ExternalOutput").ap()

    bc_dram = nc.dram_tensor("bc_dram", [32, T_SC], F32, kind="Internal").ap()

    es = ExitStack()
    with es:
        sb = lambda n, sh, dt: es.enter_context(nc.sbuf_tensor(n, sh, dt))
        ps = lambda n, sh, dt: es.enter_context(nc.psum_tensor(n, sh, dt))

        xTb = sb("xTb", [128, 16 * T_SC], F32)
        xTh_s = sb("xTh_s", [128, G * T_SC], F32)
        xprojs = sb("xprojs", [128, 16 * 160], F32)
        dtws = sb("dtws", [128, DH], F32)
        a_neg = sb("a_neg", [128, G * 16], F32)
        d_s = sb("d_s", [128, G], F32)
        b_s = sb("b_s", [128, G], F32)
        dtT = sb("dtT", [128, T_SC], F32)
        bcc = sb("bcc", [32, T_SC], F32)
        brep = sb("brep", [128, 16 * T_SC], F32)
        crep = sb("crep", [128, 16 * T_SC], F32)
        delta = [sb(f"delta{i}", [128, T_SC], F32) for i in range(2)]
        etmp = sb("etmp", [128, T_SC], F32)
        u_g = [sb(f"u{i}", [128, T_SC], F32) for i in range(2)]
        NB = 4
        dA = [sb(f"dA{i}", [128, T_SC], F32) for i in range(NB)]
        dBx = [sb(f"dBx{i}", [128, T_SC], F32) for i in range(NB)]
        h_b = [sb(f"h{i}", [128, T_SC], F32) for i in range(NB)]
        hc = sb("hc", [128, T_SC], F32)
        y_g = [sb(f"y{i}", [128, T_SC], F32) for i in range(2)]
        carry = sb("carry", [128, G * 16], F32)

        p_dt = ps("p_dt", [128, T_SC], F32)
        p_bc = ps("p_bc", [32, T_SC], F32)
        p_z = [ps(f"p_z{i}", [128, T_SC], F32) for i in range(2)]

        sm = Sems(es, nc, ["ld", "mm", "sc", "ve", "gp", "st", "bc"])
        M = {}

        consts_emit = setup_consts(nc, es)

        with nc.Block() as blk:
            @blk.vector
            def _(v):
                consts_emit(v)
                v.memset(carry[:], 0.0)
                sm.inc(v.memset(hc[:], 0.0), "ve")

            prep_ve = sm.val("ve")

            @blk.sync
            def _(s):
                sm.inc(s.dma_start(
                    out=xprojs[:].rearrange("p (g e) -> p g e", e=160),
                    in_=xprojT.rearrange("(g p) e -> p g e", p=128)), "ld", 16)
                sm.inc(s.dma_start(out=dtws[:], in_=dtwT), "ld", 16)
                sm.inc(s.dma_start(out=a_neg[:], in_=a_log), "ld", 16)
                sm.inc(s.dma_start(out=d_s[:], in_=d_prm), "ld", 16)
                sm.inc(s.dma_start(out=b_s[:], in_=b_dt), "ld", 16)

            prep_ld = sm.val("ld")

            @blk.scalar
            def _(s):
                s.wait_ge(sm.h["ld"], prep_ld)
                s.wait_ge(sm.h["ve"], prep_ve)
                sm.inc(s.activation(out=a_neg[:], in_=a_neg[:], func=AF.Exp), "sc")

            prep_sc = sm.val("sc")

            @blk.vector
            def _(v):
                v.wait_ge(sm.h["sc"], prep_sc)
                sm.inc(v.tensor_scalar_mul(a_neg[:], a_neg[:], -1.0), "ve")

            prep_ve2 = sm.val("ve")

            for c in range(N_CHUNK):
                @blk.sync
                def _(s, c=c):
                    if c >= 1:
                        s.wait_ge(sm.h["mm"], M[("mmx", c - 1)])
                        s.wait_ge(sm.h["ve"], M[("vend", c - 1)])
                        s.wait_ge(sm.h["sc"], M[("scend", c - 1)])
                        s.wait_ge(sm.h["gp"], M[("gpend", c - 1)])
                    sm.inc(s.dma_start(
                        out=xTb[:].rearrange("p (g t) -> p g t", t=T_SC),
                        in_=xT.rearrange("(g p) t -> p g t", p=128)[:, :, c * T_SC:(c + 1) * T_SC]),
                        "ld", 16)
                    sm.inc(s.dma_start(
                        out=xTh_s[:].rearrange("p (g t) -> p g t", t=T_SC),
                        in_=xTh.rearrange("(g p) t -> p g t", p=128)[:, :, c * T_SC:(c + 1) * T_SC]),
                        "ld", 16)

                M[("ldx", c)] = sm.val("ld")

                @blk.tensor
                def _(t, c=c):
                    t.wait_ge(sm.h["ld"], M[("ldx", c)])
                    if c == 0:
                        t.wait_ge(sm.h["ve"], prep_ve2)
                    else:
                        t.wait_ge(sm.h["sc"], M[("scdt", c - 1)])
                    for k in range(16):
                        t.matmul(p_dt[:], xprojs[:, k * 160:k * 160 + 128],
                                 xTb[:, k * T_SC:(k + 1) * T_SC],
                                 start=(k == 0), stop=(k == 15))
                    for k in range(16):
                        ins = t.matmul(p_bc[:], xprojs[:, k * 160 + 128:(k + 1) * 160],
                                       xTb[:, k * T_SC:(k + 1) * T_SC],
                                       start=(k == 0), stop=(k == 15))
                    sm.inc(ins, "mm")

                M[("mmx", c)] = sm.val("mm")

                @blk.scalar
                def _(s, c=c):
                    s.wait_ge(sm.h["mm"], M[("mmx", c)])
                    if c >= 1:
                        s.wait_ge(sm.h["st"], M[("stbc", c - 1)])
                        s.wait_ge(sm.h["mm"], M[("mmz", c - 1)])
                    s.copy(dtT[:], p_dt[:])
                    sm.inc(s.copy(bcc[:], p_bc[:]), "sc")

                M[("scdt", c)] = sm.val("sc")

                @blk.sync
                def _(s, c=c):
                    s.wait_ge(sm.h["sc"], M[("scdt", c)])
                    if c >= 1:
                        s.wait_ge(sm.h["bc"], M[("bcr", c - 1)])
                    sm.inc(s.dma_start(out=bc_dram, in_=bcc[:]), "st", 16)
                    M[("stbc", c)] = sm.val("st")
                    s.wait_ge(sm.h["st"], M[("stbc", c)])
                    if c >= 1:
                        s.wait_ge(sm.h["gp"], M[("gpend", c - 1)])
                        s.wait_ge(sm.h["ve"], M[("vend", c - 1)])
                    sm.inc(s.dma_start(
                        out=brep[:].rearrange("p (n t) -> p n t", t=T_SC),
                        in_=bass.AP(bc_dram.tensor, 0, [[0, 128], [T_SC, 16], [1, T_SC]])),
                        "bc", 16)
                    sm.inc(s.dma_start(
                        out=crep[:].rearrange("p (n t) -> p n t", t=T_SC),
                        in_=bass.AP(bc_dram.tensor, 16 * T_SC, [[0, 128], [T_SC, 16], [1, T_SC]])),
                        "bc", 16)

                M[("bcr", c)] = sm.val("bc")

                for g in range(G):
                    gb = g % 2

                    @blk.tensor
                    def _(t, c=c, g=g, gb=gb):
                        if g < 2:
                            t.wait_ge(sm.h["sc"], M[("scdt", c)])
                        if (c, g) > (0, 1):
                            t.wait_ge(sm.h["sc"], M[("scz", c, g)])
                        ins = t.matmul(p_z[gb][:], dtws[:, g * 128:(g + 1) * 128],
                                       dtT[:], start=True, stop=True)
                        sm.inc(ins, "mm")

                    M[("mmz", c)] = sm.val("mm")
                    mm_z = sm.val("mm")

                    scB = sm.val("sc")
                    gpB = sm.val("gp")
                    veB = sm.val("ve")

                    pv = M.get(("veB", c, g - 1) if g >= 1 else ("veB", c - 1, G - 1))
                    pg_ = M.get(("gpB", c, g - 1) if g >= 1 else ("gpB", c - 1, G - 1))
                    M[("veB", c, g)] = veB
                    M[("gpB", c, g)] = gpB

                    @blk.scalar
                    def _(s, c=c, g=g, gb=gb, mm_z=mm_z, veB=veB, pv=pv):
                        s.wait_ge(sm.h["mm"], mm_z)
                        if (c, g) > (0, 1):
                            s.wait_ge(sm.h["ve"], M[("veu", c, g)])
                        s.activation(out=etmp[:], in_=p_z[gb][:], func=AF.Exp,
                                     bias=b_s[:, g:g + 1], scale=1.0)
                        sm.inc(s.activation(out=delta[gb][:], in_=etmp[:],
                                            func=AF.Ln, bias=1.0, scale=1.0), "sc")
                        s.wait_ge(sm.h["st"], M.get(("sty", c, g), 0))
                        sm.inc(s.activation(out=y_g[gb][:],
                                            in_=xTh_s[:, g * T_SC:(g + 1) * T_SC],
                                            func=AF.Copy, scale=d_s[:, g:g + 1]), "sc")
                        for n in range(N_ST):
                            nb = n % NB
                            if n >= NB:
                                s.wait_ge(sm.h["ve"], veB + 1 + n - NB + 1)
                            elif pv is not None:
                                s.wait_ge(sm.h["ve"], pv + 16 - NB + n + 1)
                            sm.inc(s.activation(out=dA[nb][:], in_=delta[gb][:],
                                                func=AF.Exp,
                                                scale=a_neg[:, (g * 16 + n):(g * 16 + n) + 1]),
                                   "sc")

                    if g + 2 < G:
                        nk = (c, g + 2)
                    else:
                        nk = (c + 1, g + 2 - G)
                    M[("scz",) + nk] = sm.val("sc")

                    @blk.vector
                    def _(v, c=c, g=g, gb=gb, scB=scB):
                        v.wait_ge(sm.h["sc"], scB + 1)
                        sm.inc(v.tensor_mul(u_g[gb][:], delta[gb][:],
                                            xTh_s[:, g * T_SC:(g + 1) * T_SC]), "ve")

                    M[("veu",) + nk] = sm.val("ve")
                    ve_u = sm.val("ve")
                    veB = sm.val("ve")
                    M[("veB", c, g)] = veB

                    @blk.gpsimd
                    def _(gp, c=c, g=g, gb=gb, ve_u=ve_u, veB=veB, pv=pv):
                        gp.wait_ge(sm.h["ve"], ve_u)
                        gp.wait_ge(sm.h["bc"], M[("bcr", c)])

                        def em_dbx(n):
                            nb = n % NB
                            if n >= NB:
                                gp.wait_ge(sm.h["ve"], veB + n - NB + 1)
                            sm.inc(gp.tensor_mul(dBx[nb][:], u_g[gb][:],
                                                 brep[:, n * T_SC:(n + 1) * T_SC]), "gp")

                        def em_carry(j):
                            nb = j % NB
                            gp.wait_ge(sm.h["ve"], veB + j + 1)
                            sm.inc(gp.tensor_copy(carry[:, (g * 16 + j):(g * 16 + j) + 1],
                                                  h_b[nb][:, T_SC - 1:T_SC]), "gp")

                        for n in range(NB):
                            em_dbx(n)
                        for j in range(N_ST - NB):
                            em_carry(j)
                            em_dbx(j + NB)
                        for j in range(N_ST - NB, N_ST):
                            em_carry(j)

                    M[("gpend", c)] = sm.val("gp")

                    @blk.vector
                    def _(v, c=c, g=g, gb=gb, scB=scB, gpB=gpB, veB=veB, pg_=pg_):
                        if pg_ is not None:
                            v.wait_ge(sm.h["gp"], pg_ + 32)
                        for n in range(N_ST):
                            nb = n % NB
                            v.wait_ge(sm.h["sc"], scB + 2 + n + 1)
                            pos = n + 1 if n < NB else 2 * n - 2
                            v.wait_ge(sm.h["gp"], gpB + pos)
                            init = 0.0 if c == 0 else carry[:, (g * 16 + n):(g * 16 + n) + 1]
                            v.tensor_tensor_scan(out=h_b[nb][:], data0=dA[nb][:],
                                                 data1=dBx[nb][:], initial=init,
                                                 op0=ALU.mult, op1=ALU.add)
                            v.tensor_mul(hc[:], h_b[nb][:],
                                         crep[:, n * T_SC:(n + 1) * T_SC])
                            sm.inc(v.tensor_add(y_g[gb][:], y_g[gb][:], hc[:]), "ve")

                    ve_all_n = sm.val("ve")

                    @blk.sync
                    def _(s, c=c, g=g, gb=gb, ve_all_n=ve_all_n):
                        s.wait_ge(sm.h["ve"], ve_all_n)
                        sm.inc(s.dma_start(
                            out=y_out[g * 128:(g + 1) * 128, c * T_SC:(c + 1) * T_SC],
                            in_=y_g[gb][:]), "st", 16)

                    M[("sty",) + nk] = sm.val("st")

                M[("vend", c)] = sm.val("ve")
                M[("scend", c)] = sm.val("sc")

            @blk.sync
            def _(s):
                s.wait_ge(sm.h["st"], sm.val("st"))

    return nc


def scan_inputs(inputs, c):
    b, p = c // 2, c % 2
    x = np.asarray(inputs["L0_attn_x"][b])
    xT = np.ascontiguousarray(x.T)
    dh = slice(p * 1024, (p + 1) * 1024)
    xTh = np.ascontiguousarray(xT[dh])
    xprojT = np.ascontiguousarray(np.asarray(inputs["L0_attn_x_proj_w"]).T)
    dtwT = np.ascontiguousarray(np.asarray(inputs["L0_attn_dt_proj_w"]).T[:, dh])
    a_log = np.ascontiguousarray(
        np.asarray(inputs["L0_attn_A_log"])[dh].reshape(8, 128, 16)
        .transpose(1, 0, 2).reshape(128, 8 * 16))
    d_prm = np.ascontiguousarray(
        np.asarray(inputs["L0_attn_D_param"])[dh].reshape(8, 128).T)
    b_dt = np.ascontiguousarray(
        np.asarray(inputs["L0_attn_dt_proj_b"])[dh].reshape(8, 128).T)
    return dict(xT=xT, xTh=xTh, xprojT=xprojT, dtwT=dtwT,
                a_log=a_log, d_prm=d_prm, b_dt=b_dt)


def run_scan_phase(inputs):
    nc = build_scan(G=8)
    in_maps = [scan_inputs(inputs, c) for c in range(8)]
    res = run_bass_kernel_spmd(nc, in_maps, core_ids=list(range(8)))
    y = np.empty((4, L_SEQ, D_MODEL), np.float32)
    for c in range(8):
        b, p = c // 2, c % 2
        y[b, :, p * 1024:(p + 1) * 1024] = np.asarray(res.results[c]["y_out"]).T
    return y





TOK = 2048
TC = 512
N_TC = TOK // TC
SCALE_ATT = 0.08838834764831845

def build_blocks(dbg=False):
    nc = bass.Bass("TRN2", target_bir_lowering=False, dynamic_dma_scratch_size=8192)

    ein = lambda n, sh, dt: nc.dram_tensor(n, sh, dt, kind="ExternalInput").ap()
    x_res = ein("x_res", [D_MODEL, TOK], F32)
    y_own = ein("y_own", [D_MODEL, TOK], F32)
    wg_u8 = ein("wg_u8", [D_MODEL, 8192], U8)
    wu_u8 = ein("wu_u8", [D_MODEL, 8192], U8)
    wd_u8 = ein("wd_u8", [8192, D_MODEL], U8)
    sg = ein("sg", [128, 64], F32)
    su = ein("su", [128, 64], F32)
    sd = ein("sd", [128, 16], F32)
    wqT = ein("wqT", [D_MODEL, D_MODEL], F32)
    woT = ein("woT", [D_MODEL, D_MODEL], F32)
    kT = ein("kT", [D_MODEL, 4096], F32)
    v_in = ein("v_in", [4096, D_MODEL], F32)
    g1_u8 = ein("g1_u8", [D_MODEL, 5632], U8)
    u1_u8 = ein("u1_u8", [D_MODEL, 5632], U8)
    d1_u8 = ein("d1_u8", [5632, D_MODEL], U8)
    s1g = ein("s1g", [128, 44], F32)
    s1u = ein("s1u", [128, 44], F32)
    s1d = ein("s1d", [128, 16], F32)

    outT = nc.dram_tensor("outT", [D_MODEL, TOK], F32, kind="ExternalOutput").ap()
    dbg_out = (nc.dram_tensor("dbg_out", [8, D_MODEL, TC], F32, kind="ExternalOutput").ap()
               if dbg else None)

    dint = lambda n, sh, dt: nc.dram_tensor(n, sh, dt, kind="Internal").ap()
    wgB = dint("wgB", [D_MODEL, 8192], BF16)
    wuB = dint("wuB", [D_MODEL, 8192], BF16)
    wdB = dint("wdB", [8192, D_MODEL], BF16)
    g1B = dint("g1B", [D_MODEL, 5632], BF16)
    u1B = dint("u1B", [D_MODEL, 5632], BF16)
    d1B = dint("d1B", [5632, D_MODEL], BF16)
    wqB = dint("wqB", [D_MODEL, D_MODEL], BF16)
    woB = dint("woB", [D_MODEL, D_MODEL], BF16)
    kTB = dint("kTB", [D_MODEL, 4096], BF16)
    vB = dint("vB", [4096, D_MODEL], BF16)

    es = ExitStack()
    with es:
        sb = lambda n, sh, dt: es.enter_context(nc.sbuf_tensor(n, sh, dt))
        ps = lambda n, sh, dt: es.enter_context(nc.psum_tensor(n, sh, dt))

        r_s = sb("r_s", [128, 16 * TC], F32)
        lnf = sb("lnf", [128, 16 * TC], F32)
        lnb = sb("lnb", [128, 16 * TC], BF16)
        arena = sb("arena", [128, 64 * TC], BF16)
        ar32 = arena[:].bitcast(F32)
        wgu = [sb(f"wgu{i}", [128, 16 * 128], BF16) for i in range(4)]
        wdn = [sb(f"wdn{i}", [128, 64 * 128], BF16) for i in range(1)]
        kbl = [sb(f"kbl{i}", [128, 4096], BF16) for i in range(1)]
        vbl = [sb(f"vbl{i}", [128, 32 * 128], BF16) for i in range(1)]
        gsil = [sb(f"gsil{i}", [128, TC], BF16) for i in range(2)]
        usil = [sb(f"usil{i}", [128, TC], BF16) for i in range(2)]
        recb = sb("recb", [128, TC], F32)
        sg_s = sb("sg_s", [128, 64], F32)
        su_s = sb("su_s", [128, 64], F32)
        sd_s = sb("sd_s", [128, 16], F32)
        s1g_s = sb("s1g_s", [128, 44], F32)
        s1u_s = sb("s1u_s", [128, 44], F32)
        s1d_s = sb("s1d_s", [128, 16], F32)
        onesc_b = sb("onesc_b", [128, 1], BF16)
        onesc_f = sb("onesc_f", [128, 1], F32)
        onesr_f = sb("onesr_f", [1, 128], F32)
        ones128 = sb("ones128", [128, 128], BF16)
        m128r = sb("m128r", [1, 128], BF16)
        csb = sb("csb", [1, TC], BF16)
        csb2 = sb("csb2", [1, TC], BF16)
        stt1 = sb("stt1", [1, TC], F32)
        stt2 = sb("stt2", [1, TC], F32)
        stt3 = sb("stt3", [1, TC], F32)
        stt4 = sb("stt4", [1, TC], F32)

        pA = ps("pA", [128, TC], F32)
        pB = ps("pB", [128, TC], F32)
        pC = ps("pC", [128, TC], F32)
        pD = ps("pD", [128, TC], F32)
        pE = ps("pE", [128, TC], F32)
        pDen = ps("pDen", [128, TC], F32)
        pS1 = ps("pS1", [1, TC], F32)
        pS2 = ps("pS2", [1, TC], F32)

        sm = Sems(es, nc, ["ld", "mm", "sc", "ve", "gp", "st"])
        M = {}
        consts_emit = setup_consts(nc, es)

        with nc.Block() as blk:
            @blk.vector
            def _(v):
                consts_emit(v)
                v.memset(onesc_b[:], 1.0)
                v.memset(onesc_f[:], 1.0)
                v.memset(onesr_f[:], 1.0)
                v.memset(ones128[:], 1.0)
                sm.inc(v.memset(m128r[:], -128.0), "ve")

            prep_ve = sm.val("ve")

            @blk.gpsimd
            def _(g):
                for src, dst in ((wg_u8, wgB), (wu_u8, wuB), (wd_u8, wdB),
                                 (wqT, wqB), (kT, kTB), (v_in, vB), (woT, woB),
                                 (g1_u8, g1B), (u1_u8, u1B), (d1_u8, d1B)):
                    sm.inc(g.dma_start(out=dst, in_=src), "gp", 16)

            prep_gp = sm.val("gp")

            @blk.sync
            def _(s):
                for src, dst in ((sg, sg_s), (su, su_s), (sd, sd_s),
                                 (s1g, s1g_s), (s1u, s1u_s), (s1d, s1d_s)):
                    sm.inc(s.dma_start(out=dst[:], in_=src), "ld", 16)

            prep_ld = sm.val("ld")

            def emit_ln(tag, tc, last_mm, last_ve, last_sc, final=False):
                @blk.scalar
                def _(s):
                    s.wait_ge(sm.h["ve"], last_ve)
                    sm.inc(s.activation(out=ar32[:, 0:16 * TC], in_=r_s[:],
                                        func=AF.Square), "sc")
                sq_sc = sm.val("sc")

                @blk.tensor
                def _(t):
                    t.wait_ge(sm.h["ve"], last_ve)
                    t.wait_ge(sm.h["mm"], last_mm)
                    for k in range(16):
                        t.matmul(pS1[:], onesc_f[:], r_s[:, k * TC:(k + 1) * TC],
                                 start=(k == 0), stop=(k == 15))
                    t.wait_ge(sm.h["sc"], sq_sc)
                    for k in range(16):
                        ins = t.matmul(pS2[:], onesc_f[:],
                                       ar32[:, k * TC:(k + 1) * TC],
                                       start=(k == 0), stop=(k == 15))
                    sm.inc(ins, "mm")
                st_mm = sm.val("mm")

                @blk.vector
                def _(v):
                    v.wait_ge(sm.h["mm"], st_mm)
                    v.tensor_scalar_mul(stt1[:], pS1[:], 1.0 / 2048.0)
                    v.tensor_scalar_mul(stt4[:], pS2[:], 1.0 / 2048.0)
                    v.tensor_mul(stt2[:], stt1[:], stt1[:])
                    sm.inc(v.tensor_sub(stt4[:], stt4[:], stt2[:]), "ve")

                var_ve = sm.val("ve")

                @blk.scalar
                def _(s):
                    s.wait_ge(sm.h["ve"], var_ve)
                    s.activation(out=stt2[:], in_=stt4[:], func=AF.Ln,
                                 bias=1e-5, scale=1.0)
                    sm.inc(s.activation(out=stt2[:], in_=stt2[:], func=AF.Exp,
                                        scale=-0.5), "sc")

                rstd_sc = sm.val("sc")

                @blk.vector
                def _(v):
                    v.wait_ge(sm.h["sc"], rstd_sc)
                    v.tensor_mul(stt3[:], stt1[:], stt2[:])
                    sm.inc(v.tensor_scalar_mul(stt3[:], stt3[:], -1.0), "ve")

                m1_ve = sm.val("ve")

                @blk.tensor
                def _(t):
                    t.wait_ge(sm.h["ve"], m1_ve)
                    t.matmul(pA[:], onesr_f[:], stt2[:], start=True, stop=True)
                    sm.inc(t.matmul(pB[:], onesr_f[:], stt3[:],
                                    start=True, stop=True), "mm")

                bc_mm = sm.val("mm")

                @blk.vector
                def _(v):
                    v.wait_ge(sm.h["mm"], bc_mm)
                    bA = bass.AP(pA.tensor if hasattr(pA, "tensor") else pA._handle, 0, []) if False else None
                    v.tensor_tensor(
                        out=lnf[:].rearrange("p (a t) -> p a t", t=TC),
                        in0=r_s[:].rearrange("p (a t) -> p a t", t=TC),
                        in1=pA[:].unsqueeze(1).broadcast_to([128, 16, TC]),
                        op=ALU.mult)
                    sm.inc(v.tensor_tensor(
                        out=lnf[:].rearrange("p (a t) -> p a t", t=TC),
                        in0=lnf[:].rearrange("p (a t) -> p a t", t=TC),
                        in1=pB[:].unsqueeze(1).broadcast_to([128, 16, TC]),
                        op=ALU.add), "ve")

                ln_ve = sm.val("ve")
                if final:
                    return dict(mm=sm.val("mm"), ve=ln_ve, sc=sm.val("sc"))

                @blk.scalar
                def _(s):
                    s.wait_ge(sm.h["ve"], ln_ve)
                    sm.inc(s.copy(lnb[:], lnf[:]), "sc")
                lnb_sc = sm.val("sc")

                @blk.tensor
                def _(t):
                    t.wait_ge(sm.h["sc"], lnb_sc)
                    for k in range(16):
                        ins = t.matmul(pS1[:], onesc_b[:], lnb[:, k * TC:(k + 1) * TC],
                                       start=(k == 0), stop=(k == 15))
                    sm.inc(ins, "mm")
                cs_mm = sm.val("mm")

                @blk.scalar
                def _(s):
                    s.wait_ge(sm.h["mm"], cs_mm)
                    sm.inc(s.copy(csb[:], pS1[:]), "sc")

                return dict(mm=sm.val("mm"), ve=ln_ve, sc=sm.val("sc"),
                            lnb_sc=lnb_sc, cs_sc=sm.val("sc"))

            def emit_ffn(tag, tc, n_it, wgt, wut, wdt, sgA, suA, sdA, lnm):
                pg = [pA, pC]
                pu = [pB, pD]
                mm0 = sm.val("mm")
                sc0 = sm.val("sc")
                ve0 = sm.val("ve")
                mm_git = {}
                for it in range(n_it):
                    ib = it % 2

                    @blk.sync
                    def _(s, it=it, ib=ib, mm0=mm0):
                        s.wait_ge(sm.h["gp"], prep_gp)
                        if it >= 2:
                            s.wait_ge(sm.h["mm"], mm_git[it - 2])
                        else:
                            s.wait_ge(sm.h["mm"], mm0)
                        sm.inc(s.dma_start(
                            out=wgu[ib][:].rearrange("p (a m) -> p a m", m=128),
                            in_=wgt.rearrange("(a p) i -> p a i", p=128)[:, :, it * 128:(it + 1) * 128]),
                            "ld", 16)
                        sm.inc(s.dma_start(
                            out=wgu[2 + ib][:].rearrange("p (a m) -> p a m", m=128),
                            in_=wut.rearrange("(a p) i -> p a i", p=128)[:, :, it * 128:(it + 1) * 128]),
                            "ld", 16)

                    ld_it = sm.val("ld")

                    @blk.tensor
                    def _(t, it=it, ib=ib, ld_it=ld_it, sc0=sc0):
                        t.wait_ge(sm.h["ld"], ld_it)
                        if it == 0:
                            t.wait_ge(sm.h["sc"], lnm["cs_sc"])
                        if it >= 2:
                            t.wait_ge(sm.h["sc"], sc0 + it - 1)
                        for k in range(16):
                            t.matmul(pg[ib][:], wgu[ib][:, k * 128:(k + 1) * 128],
                                     lnb[:, k * TC:(k + 1) * TC],
                                     start=(k == 0), stop=False)
                        t.matmul(pg[ib][:], m128r[:], csb[:], start=False, stop=True)
                        for k in range(16):
                            t.matmul(pu[ib][:], wgu[2 + ib][:, k * 128:(k + 1) * 128],
                                     lnb[:, k * TC:(k + 1) * TC],
                                     start=(k == 0), stop=False)
                        sm.inc(t.matmul(pu[ib][:], m128r[:], csb[:],
                                        start=False, stop=True), "mm")

                    mm_git[it] = sm.val("mm")

                @blk.scalar
                def _(s, n_it=n_it, mm0=mm0, ve0=ve0):
                    s.wait_ge(sm.h["ve"], ve0)
                    for it in range(n_it):
                        ib = it % 2
                        s.wait_ge(sm.h["mm"], mm_git[it])
                        if it >= 2:
                            s.wait_ge(sm.h["ve"], ve0 + it - 1)
                        s.activation(out=gsil[ib][:], in_=pg[ib][:], func=AF.Silu,
                                     scale=sgA[:, it:it + 1])
                        sm.inc(s.activation(out=usil[ib][:], in_=pu[ib][:],
                                            func=AF.Copy, scale=suA[:, it:it + 1]), "sc")

                @blk.vector
                def _(v, n_it=n_it, sc0=sc0):
                    for it in range(n_it):
                        ib = it % 2
                        v.wait_ge(sm.h["sc"], sc0 + it + 1)
                        sm.inc(v.tensor_mul(arena[:, it * TC:(it + 1) * TC],
                                            gsil[ib][:], usil[ib][:]), "ve")

                ve_h_all = sm.val("ve")

                @blk.tensor
                def _(t, n_it=n_it, ve_h_all=ve_h_all):
                    t.wait_ge(sm.h["ve"], ve_h_all)
                    t.wait_ge(sm.h["ve"], lnm["ve"])
                    t.wait_ge(sm.h["sc"], lnm["cs_sc"])
                    for it in range(n_it):
                        ins = t.matmul(pS1[:], onesc_b[:],
                                       arena[:, it * TC:(it + 1) * TC],
                                       start=(it == 0), stop=(it == n_it - 1))
                    sm.inc(ins, "mm")

                csh_mm = sm.val("mm")

                @blk.scalar
                def _(s, csh_mm=csh_mm):
                    s.wait_ge(sm.h["mm"], csh_mm)
                    sm.inc(s.copy(csb2[:], pS1[:]), "sc")

                csh_sc = sm.val("sc")

                pdn = [pA, pC]
                mm_dot = {}
                ve1 = sm.val("ve")
                for ot in range(16):
                    @blk.sync
                    def _(s, ot=ot, csh_mm=csh_mm):
                        if ot >= 1:
                            s.wait_ge(sm.h["mm"], mm_dot[ot - 1])
                        else:
                            s.wait_ge(sm.h["mm"], csh_mm)
                        sm.inc(s.dma_start(
                            out=wdn[0][:, 0:n_it * 128].rearrange("p (a m) -> p a m", m=128),
                            in_=wdt.rearrange("(a p) m -> p a m", p=128)[:, 0:n_it, ot * 128:(ot + 1) * 128]),
                            "ld", 16)

                    ld_ot = sm.val("ld")

                    @blk.tensor
                    def _(t, ot=ot, ld_ot=ld_ot, csh_sc=csh_sc, ve1=ve1, n_it=n_it):
                        pb = ot % 2
                        t.wait_ge(sm.h["ld"], ld_ot)
                        t.wait_ge(sm.h["sc"], csh_sc)
                        if ot >= 2:
                            t.wait_ge(sm.h["ve"], ve1 + ot - 1)
                        for k in range(n_it):
                            t.matmul(pdn[pb][:], wdn[0][:, k * 128:(k + 1) * 128],
                                     arena[:, k * TC:(k + 1) * TC],
                                     start=(k == 0), stop=False)
                        sm.inc(t.matmul(pdn[pb][:], m128r[:], csb2[:],
                                        start=False, stop=True), "mm")

                    mm_dot[ot] = sm.val("mm")

                @blk.vector
                def _(v, sdA=sdA):
                    for ot in range(16):
                        pb = ot % 2
                        v.wait_ge(sm.h["mm"], mm_dot[ot])
                        sm.inc(v.scalar_tensor_tensor(
                            out=r_s[:, ot * TC:(ot + 1) * TC],
                            in0=pdn[pb][:], scalar=sdA[:, ot:ot + 1],
                            in1=lnf[:, ot * TC:(ot + 1) * TC],
                            op0=ALU.mult, op1=ALU.add), "ve")

                return dict(mm=sm.val("mm"), ve=sm.val("ve"), sc=sm.val("sc"))

            def emit_attn(tag, tc, lnm):
                qt = arena[:, 0:16 * TC]
                otn = arena[:, 16 * TC:32 * TC]
                expT = arena[:, 32 * TC:64 * TC]
                pq = [pA, pC]
                mm0 = sm.val("mm")
                sc0 = sm.val("sc")
                mm_qot = {}
                for ot in range(16):
                    ob = ot % 2

                    @blk.sync
                    def _(s, ot=ot, ob=ob, mm0=mm0):
                        s.wait_ge(sm.h["gp"], prep_gp)
                        if ot >= 2:
                            s.wait_ge(sm.h["mm"], mm_qot[ot - 2])
                        else:
                            s.wait_ge(sm.h["mm"], mm0)
                        sm.inc(s.dma_start(
                            out=wgu[ob][:].rearrange("p (a m) -> p a m", m=128),
                            in_=wqB.rearrange("(a p) m -> p a m", p=128)[:, :, ot * 128:(ot + 1) * 128]),
                            "ld", 16)

                    ld_ot = sm.val("ld")

                    @blk.tensor
                    def _(t, ot=ot, ob=ob, ld_ot=ld_ot, sc0=sc0):
                        t.wait_ge(sm.h["ld"], ld_ot)
                        if ot == 0:
                            t.wait_ge(sm.h["sc"], lnm["lnb_sc"])
                            t.wait_ge(sm.h["mm"], lnm["mm"])
                        if ot >= 2:
                            t.wait_ge(sm.h["sc"], sc0 + ot - 1)
                        for k in range(16):
                            ins = t.matmul(pq[ob][:], wgu[ob][:, k * 128:(k + 1) * 128],
                                           lnb[:, k * TC:(k + 1) * TC],
                                           start=(k == 0), stop=(k == 15))
                        sm.inc(ins, "mm")

                    mm_qot[ot] = sm.val("mm")

                @blk.scalar
                def _(s, sc0=sc0):
                    for ot in range(16):
                        ob = ot % 2
                        s.wait_ge(sm.h["mm"], mm_qot[ot])
                        sm.inc(s.copy(qt[:, ot * TC:(ot + 1) * TC], pq[ob][:]), "sc")

                qt_sc = sm.val("sc")

                psc = [pB, pD]
                mm_av = {}
                M[("rec", tag, -1)] = sm.val("sc")
                M[("otn", tag, -1)] = sm.val("ve")
                for hd in range(16):
                    @blk.sync
                    def _(s, hd=hd, mm0=mm0):
                        if hd >= 1:
                            s.wait_ge(sm.h["mm"], mm_av[hd - 1])
                        else:
                            s.wait_ge(sm.h["mm"], mm_qot[15])
                        sm.inc(s.dma_start(out=kbl[0][:],
                                    in_=kTB[hd * 128:(hd + 1) * 128, :]), "ld", 16)
                        sm.inc(s.dma_start(
                            out=vbl[0][:].rearrange("p (a m) -> p a m", m=128),
                            in_=vB.rearrange("(a p) m -> p a m", p=128)[:, :, hd * 128:(hd + 1) * 128]),
                            "ld", 16)

                    ld_hd = sm.val("ld")
                    mmS = sm.val("mm")
                    scE = sm.val("sc")

                    @blk.tensor
                    def _(t, hd=hd, ld_hd=ld_hd, scE=scE, qt_sc=qt_sc):
                        t.wait_ge(sm.h["ld"], ld_hd)
                        if hd == 0:
                            t.wait_ge(sm.h["sc"], qt_sc)
                        for kvt in range(32):
                            kb = kvt % 2
                            if kvt >= 2:
                                t.wait_ge(sm.h["sc"], scE + kvt - 1)
                            sm.inc(t.matmul(psc[kb][:],
                                            kbl[0][:, kvt * 128:(kvt + 1) * 128],
                                            qt[:, hd * TC:(hd + 1) * TC],
                                            start=True, stop=True), "mm")

                    @blk.scalar
                    def _(s, hd=hd, mmS=mmS, tag=tag):
                        s.wait_ge(sm.h["mm"], M.get(("avm", tag, hd - 1), 0))
                        for kvt in range(32):
                            kb = kvt % 2
                            s.wait_ge(sm.h["mm"], mmS + kvt + 1)
                            sm.inc(s.activation(out=expT[:, kvt * TC:(kvt + 1) * TC],
                                                in_=psc[kb][:], func=AF.Exp,
                                                scale=SCALE_ATT, bias=-20.0), "sc")

                    @blk.tensor
                    def _(t, hd=hd, scE=scE, tag=tag):
                        t.wait_ge(sm.h["sc"], M[("rec", tag, hd - 1)])
                        t.wait_ge(sm.h["ve"], M[("otn", tag, hd - 1)])
                        for kvt in range(32):
                            t.wait_ge(sm.h["sc"], scE + kvt + 1)
                            t.matmul(pDen[:], ones128[:],
                                     expT[:, kvt * TC:(kvt + 1) * TC],
                                     start=(kvt == 0), stop=(kvt == 31))
                            ins = t.matmul(pE[:], vbl[0][:, kvt * 128:(kvt + 1) * 128],
                                           expT[:, kvt * TC:(kvt + 1) * TC],
                                           start=(kvt == 0), stop=(kvt == 31))
                        sm.inc(ins, "mm")

                    M[("avm", tag, hd)] = sm.val("mm")
                    mm_av[hd] = sm.val("mm")

                    @blk.scalar
                    def _(s, hd=hd, tag=tag):
                        s.wait_ge(sm.h["mm"], mm_av[hd])
                        s.wait_ge(sm.h["ve"], M[("otn", tag, hd - 1)])
                        s.activation(out=recb[:], in_=pDen[:], func=AF.Ln)
                        sm.inc(s.activation(out=recb[:], in_=recb[:], func=AF.Exp,
                                            scale=-1.0), "sc")

                    M[("rec", tag, hd)] = sm.val("sc")
                    rec_sc = sm.val("sc")

                    @blk.vector
                    def _(v, hd=hd, rec_sc=rec_sc):
                        v.wait_ge(sm.h["sc"], rec_sc)
                        sm.inc(v.tensor_mul(otn[:, hd * TC:(hd + 1) * TC],
                                            pE[:], recb[:]), "ve")

                    M[("otn", tag, hd)] = sm.val("ve")

                otn_ve = sm.val("ve")

                pw = [pA, pC]
                mm_wot = {}
                ve1 = sm.val("ve")
                for ot in range(16):
                    ob = ot % 2

                    @blk.sync
                    def _(s, ot=ot, ob=ob):
                        if ot >= 2:
                            s.wait_ge(sm.h["mm"], mm_wot[ot - 2])
                        else:
                            s.wait_ge(sm.h["mm"], mm_av[15])
                        sm.inc(s.dma_start(
                            out=wgu[ob][:].rearrange("p (a m) -> p a m", m=128),
                            in_=woB.rearrange("(a p) m -> p a m", p=128)[:, :, ot * 128:(ot + 1) * 128]),
                            "ld", 16)

                    ld_ot = sm.val("ld")

                    @blk.tensor
                    def _(t, ot=ot, ob=ob, ld_ot=ld_ot, otn_ve=otn_ve, ve1=ve1):
                        t.wait_ge(sm.h["ld"], ld_ot)
                        t.wait_ge(sm.h["ve"], otn_ve)
                        if ot >= 2:
                            t.wait_ge(sm.h["ve"], ve1 + ot - 1)
                        for k in range(16):
                            ins = t.matmul(pw[ob][:], wgu[ob][:, k * 128:(k + 1) * 128],
                                           otn[:, k * TC:(k + 1) * TC],
                                           start=(k == 0), stop=(k == 15))
                        sm.inc(ins, "mm")

                    mm_wot[ot] = sm.val("mm")

                @blk.vector
                def _(v):
                    for ot in range(16):
                        ob = ot % 2
                        v.wait_ge(sm.h["mm"], mm_wot[ot])
                        sm.inc(v.tensor_add(r_s[:, ot * TC:(ot + 1) * TC],
                                            pw[ob][:],
                                            lnf[:, ot * TC:(ot + 1) * TC]), "ve")

                return dict(mm=sm.val("mm"), ve=sm.val("ve"), sc=sm.val("sc"))

            for tc in range(N_TC):
                @blk.sync
                def _(s, tc=tc):
                    if tc >= 1:
                        s.wait_ge(sm.h["st"], M[("out", tc - 1)])
                        s.wait_ge(sm.h["ve"], M[("rdone", tc - 1)])
                    sm.inc(s.dma_start(
                        out=lnf[:].rearrange("p (a t) -> p a t", t=TC),
                        in_=x_res.rearrange("(a p) t -> p a t", p=128)[:, :, tc * TC:(tc + 1) * TC]),
                        "ld", 16)
                    sm.inc(s.dma_start(
                        out=r_s[:].rearrange("p (a t) -> p a t", t=TC),
                        in_=y_own.rearrange("(a p) t -> p a t", p=128)[:, :, tc * TC:(tc + 1) * TC]),
                        "ld", 16)

                ld_xy = sm.val("ld")

                @blk.vector
                def _(v, tc=tc):
                    v.wait_ge(sm.h["ld"], ld_xy)
                    if tc == 0:
                        v.wait_ge(sm.h["ve"], prep_ve)
                        v.wait_ge(sm.h["ld"], prep_ld)
                    sm.inc(v.tensor_add(r_s[:], r_s[:], lnf[:]), "ve")

                r0_ve = sm.val("ve")

                def snap(idx, buf, vewait, tc=tc):
                    if not dbg or tc != 0:
                        return
                    @blk.sync
                    def _(s):
                        s.wait_ge(sm.h["ve"], vewait)
                        sm.inc(s.dma_start(
                            out=dbg_out[idx].rearrange("(a p) t -> p a t", p=128),
                            in_=buf[:].rearrange("p (a t) -> p a t", t=TC)), "st", 16)

                snap(0, r_s, r0_ve)
                ln0 = emit_ln("ln0", tc, sm.val("mm"), r0_ve, sm.val("sc"))
                snap(1, lnf, ln0["ve"])
                f0 = emit_ffn("ffn0", tc, 64, wgB, wuB, wdB, sg_s, su_s, sd_s, ln0)
                snap(2, r_s, f0["ve"])
                ln1 = emit_ln("ln1", tc, f0["mm"], f0["ve"], f0["sc"])
                snap(3, lnf, ln1["ve"])
                at = emit_attn(f"at{tc}", tc, ln1)
                snap(4, r_s, at["ve"])
                ln2 = emit_ln("ln2", tc, at["mm"], at["ve"], at["sc"])
                snap(5, lnf, ln2["ve"])
                f1 = emit_ffn("ffn1", tc, 44, g1B, u1B, d1B, s1g_s, s1u_s, s1d_s, ln2)
                snap(6, r_s, f1["ve"])
                ln3 = emit_ln("ln3", tc, f1["mm"], f1["ve"], f1["sc"], final=True)
                snap(7, lnf, ln3["ve"])
                M[("rdone", tc)] = sm.val("ve")

                @blk.sync
                def _(s, tc=tc):
                    s.wait_ge(sm.h["ve"], ln3["ve"])
                    sm.inc(s.dma_start(
                        out=outT.rearrange("(a p) t -> p a t", p=128)[:, :, tc * TC:(tc + 1) * TC],
                        in_=lnf[:].rearrange("p (a t) -> p a t", t=TC)), "st", 16)

                M[("out", tc)] = sm.val("st")

            @blk.sync
            def _(s):
                s.wait_ge(sm.h["st"], sm.val("st"))

    return nc


def _u8(w_int32_T):
    a = np.ascontiguousarray(w_int32_T)
    return np.ascontiguousarray(a.view(np.uint8).reshape(a.shape[0], a.shape[1], 4)[:, :, 0])


def blocks_inputs(inputs, c, y):
    b, p = c // 2, c % 2
    tok = slice(p * TOK, (p + 1) * TOK)
    x = np.asarray(inputs["L0_attn_x"][b])
    xT = np.ascontiguousarray(x.T[:, tok])
    yT = np.ascontiguousarray(y[b].T[:, tok])
    d = {}
    d["x_res"] = xT
    d["y_own"] = yT
    d["wg_u8"] = _u8(np.asarray(inputs["L0_ffn_gate_w"]).T)
    d["wu_u8"] = _u8(np.asarray(inputs["L0_ffn_up_w"]).T)
    d["wd_u8"] = _u8(np.asarray(inputs["L0_ffn_down_w"]).T)
    d["sg"] = np.ascontiguousarray(np.asarray(inputs["L0_ffn_gate_scale"]).reshape(64, 128).T)
    d["su"] = np.ascontiguousarray(np.asarray(inputs["L0_ffn_up_scale"]).reshape(64, 128).T)
    d["sd"] = np.ascontiguousarray(np.asarray(inputs["L0_ffn_down_scale"]).reshape(16, 128).T)
    d["wqT"] = np.ascontiguousarray(np.asarray(inputs["L1_attn_Wq"]).T)
    d["woT"] = np.ascontiguousarray(np.asarray(inputs["L1_attn_Wo"]).T)
    d["kT"] = np.ascontiguousarray(np.asarray(inputs["L1_attn_kv_k"]).reshape(4096, 2048).T)
    d["v_in"] = np.ascontiguousarray(np.asarray(inputs["L1_attn_kv_v"]).reshape(4096, 2048))
    d["g1_u8"] = _u8(np.asarray(inputs["L1_ffn_wq_gate"]).T)
    d["u1_u8"] = _u8(np.asarray(inputs["L1_ffn_wq_up"]).T)
    d["d1_u8"] = _u8(np.asarray(inputs["L1_ffn_wq_down"]).T)
    d["s1g"] = np.ascontiguousarray(np.asarray(inputs["L1_ffn_s_gate"]).reshape(44, 128).T)
    d["s1u"] = np.ascontiguousarray(np.asarray(inputs["L1_ffn_s_up"]).reshape(44, 128).T)
    d["s1d"] = np.ascontiguousarray(np.asarray(inputs["L1_ffn_s_down"]).reshape(16, 128).T)
    return d


def kernel(**inputs):
    y = run_scan_phase(inputs)
    nc = build_blocks()
    in_maps = [blocks_inputs(inputs, c, y) for c in range(8)]
    res = run_bass_kernel_spmd(nc, in_maps, core_ids=list(range(8)))
    out = np.empty((4, L_SEQ, D_MODEL), np.float32)
    for c in range(8):
        b, p = c // 2, c % 2
        out[b, p * TOK:(p + 1) * TOK, :] = np.asarray(res.results[c]["outT"]).T
    return out
